# version 11
# baseline (speedup 1.0000x reference)
import math
from contextlib import ExitStack

import numpy as np
import ml_dtypes

import concourse.bass as bass
import concourse.mybir as mybir
from concourse.bass_utils import run_bass_kernel_spmd

F32 = mybir.dt.float32
BF16 = mybir.dt.bfloat16
AF = mybir.ActivationFunctionType
ALU = mybir.AluOpType
AX = mybir.AxisListType

SEQ = 2048
DM = 2048
DKV = 512
HID = 8192
NH = 32
KVH = 8
G = NH // KVH
NT = SEQ // 128
NCH = SEQ // 512
RW = 3968
EPS = 1e-5
SCALE_H = float(np.float32(1.0 / np.sqrt(8192.0)))
SLOPES = [float(np.float32(2.0 ** (-gi / G))) for gi in range(G)]
NB = 2
FULL_ATTN = False
BAND_PAD = 1
P5LIM = 99
P4LIM = 99

ENGINES = ("pe", "act", "dve", "pool", "sp")
BLKNAME = {"pe": "tensor", "act": "scalar", "dve": "vector", "pool": "gpsimd", "sp": "sync"}

KT0, VO0, QT0 = 0, 8192, 16512
WO0 = 0
Z20, XN20, W1B0, W3B0 = 0, 8192, 16384, 24576
WKV0, WQ0, QS0, XNST0 = 0, 8192, 12288, 13312
PB0, QTB0, RB16 = 0, 6144, 14336
XN2S0 = 3072
W2PS0 = 0
XF0 = 0
XF60 = 4096
STATS0 = 6144
TRASH0 = 6208
SIL0 = 7232
LNOUT0 = 7744
B3F_COLS = 7744 + 2048
EST0, RCF0, BC0, TMP0 = 0, 9280, 7744, 8768


def stiles_for_chunk(c):
    if FULL_ATTN:
        return list(range(NT))
    lo = max(0, 4 * c - BAND_PAD)
    hi = min(NT, 4 * c + 4 + BAND_PAD)
    return list(range(lo, hi))


def tau(j):
    return ((j // G) // 2) * 4 + (j % G)


class Prog:
    def __init__(self, nc, ctx):
        self.nc = nc
        self.ctx = ctx
        self.sems = {}
        self.step = {}
        self.cnt = {}
        self.waited = {}
        self.streams = {e: [] for e in ENGINES}
        self.eq = {}
        self.bar = ()

    def queue(self, name, step=1):
        if name not in self.sems:
            self.sems[name] = self.ctx.enter_context(self.nc.semaphore(name))
            self.step[name] = step
            self.cnt[name] = 0
        return name

    def start_phase(self, tag):
        for e in ENGINES:
            self.eq[e] = self.queue(f"eng.{e}")
        self.bar = tuple((q, v) for q, v in self.cnt.items() if v > 0)

    def op(self, eng, make, deps=(), q=None, inc=True):
        deps = tuple(d for d in deps if d is not None) + self.bar
        q = q if q is not None else self.eq[eng]
        tok = None
        if inc:
            self.cnt[q] += self.step[q]
            tok = (q, self.cnt[q])
        self.streams[eng].append((make, deps, q if inc else None))
        return tok

    def dma(self, eng, slot, out_ap, in_ap, deps=()):
        return self.op(eng, lambda E, o=out_ap, i=in_ap: E.dma_start(o, i),
                       deps=deps, q=self.queue(slot, 16))

    def flush(self):
        with self.nc.Block() as block:
            for ename in ENGINES:
                if not self.streams[ename]:
                    continue

                def runner(E, ename=ename, items=self.streams[ename]):
                    for make, deps, q in items:
                        best = {}
                        for dq, dv in deps:
                            if best.get(dq, 0) < dv:
                                best[dq] = dv
                        for dq, dv in best.items():
                            key = (ename, dq)
                            if self.waited.get(key, 0) < dv:
                                E.wait_ge(self.sems[dq], dv)
                                self.waited[key] = dv
                        instr = make(E)
                        if q is not None:
                            instr.then_inc(self.sems[q], self.step[q])

                getattr(block, BLKNAME[ename])(runner)
        self.streams = {e: [] for e in ENGINES}


def build(nbatch=NB, stages=("p1", "p2", "p3", "p4", "p5")):
    nc = bass.Bass("TRN2", target_bir_lowering=False)
    TOK = nbatch * SEQ

    xin = nc.dram_tensor("xin", [TOK, DM], F32, kind="ExternalInput").ap()
    wqt = nc.dram_tensor("wqt", [DM, DM], F32, kind="ExternalInput").ap()
    wkt = nc.dram_tensor("wkt", [DM, DKV], F32, kind="ExternalInput").ap()
    wvt = nc.dram_tensor("wvt", [DM, DKV], F32, kind="ExternalInput").ap()
    wot = nc.dram_tensor("wot", [DM, DM], F32, kind="ExternalInput").ap()
    w1t = nc.dram_tensor("w1t", [DM, HID], F32, kind="ExternalInput").ap()
    w3t = nc.dram_tensor("w3t", [DM, HID], F32, kind="ExternalInput").ap()
    w2t = nc.dram_tensor("w2t", [HID, DM], F32, kind="ExternalInput").ap()
    bqc = nc.dram_tensor("bqc", [128, 16], F32, kind="ExternalInput").ap()
    bkc = nc.dram_tensor("bkc", [128, 4], F32, kind="ExternalInput").ap()
    bvc = nc.dram_tensor("bvc", [128, 16], F32, kind="ExternalInput").ap()
    rbase = nc.dram_tensor("rbase", [128, RW], BF16, kind="ExternalInput").ap()
    h128 = nc.dram_tensor("h128", [128, 128], BF16, kind="ExternalInput").ap()
    h64 = nc.dram_tensor("h64", [64, 64], BF16, kind="ExternalInput").ap()
    identin = nc.dram_tensor("identin", [128, 128], BF16, kind="ExternalInput").ap()
    onesin = nc.dram_tensor("onesin", [1, 64], BF16, kind="ExternalInput").ap()

    out = nc.dram_tensor("out", [TOK, DM], F32, kind="ExternalOutput").ap()

    qts = nc.dram_tensor("qts", [DM, SEQ], BF16, kind="Internal").ap()
    xmid = nc.dram_tensor("xmid", [TOK, DM], F32, kind="Internal").ap()
    ubuf = nc.dram_tensor("ubuf", [HID, DM], BF16, kind="Internal").ap()
    w2pt = nc.dram_tensor("w2pt", [HID, DM], BF16, kind="Internal").ap()
    w1tb = nc.dram_tensor("w1tb", [32, 128, 4096], BF16, kind="Internal").ap()
    w3tb = nc.dram_tensor("w3tb", [32, 128, 4096], BF16, kind="Internal").ap()

    with ExitStack() as ctx:
        e = ctx.enter_context
        p = Prog(nc, ctx)

        rb = e(nc.sbuf_tensor("rb", [128, RW], BF16))
        bq8 = e(nc.sbuf_tensor("bq8", [128, 16], F32))
        bkb = e(nc.sbuf_tensor("bkb", [128, 4], F32))
        bvb = e(nc.sbuf_tensor("bvb", [128, 16], F32))
        idb = e(nc.sbuf_tensor("idb", [128, 128], BF16))
        h128b = e(nc.sbuf_tensor("h128b", [128, 128], BF16))
        h64b = e(nc.sbuf_tensor("h64b", [64, 64], BF16))
        ones64 = e(nc.sbuf_tensor("ones64", [1, 64], BF16))
        BIG = e(nc.sbuf_tensor("BIG", [128, 32768], BF16))
        BIG2 = e(nc.sbuf_tensor("BIG2", [128, 32768], BF16))
        B3B = e(nc.sbuf_tensor("B3B", [128, 16384], BF16))
        B3F = e(nc.sbuf_tensor("B3F", [128, B3F_COLS], F32))
        PSA = e(nc.psum_tensor("PSA", [128, 2048], F32))
        PSB = e(nc.psum_tensor("PSB", [128, 2048], F32))

        def PA(bk):
            return PSA[:, bk * 512:(bk + 1) * 512]

        def PB(bk):
            return PSB[:, bk * 512:(bk + 1) * 512]

        def xf(i):
            return B3F[:, XF0 + i * 2048: XF0 + (i + 1) * 2048]

        XF6 = B3F[:, XF60:XF60 + 2048]

        def stats(i):
            return B3F[:, STATS0 + i * 12: STATS0 + (i + 1) * 12]

        TRASH = B3F[:, TRASH0:TRASH0 + 1024]
        SIL1 = B3F[:, SIL0:SIL0 + 512]
        LNOUT = B3F[:, LNOUT0:LNOUT0 + 2048]

        def emit_ln_stats(src_ap, sl, dep_src, dep_free, ln_state, width=2048):
            st_ = stats(sl)
            h = width // 2
            t_sum = p.op("dve", lambda E, a=src_ap, s=st_: E.reduce_sum(s[:, 0:1], a, axis=AX.X),
                         deps=[dep_src] + dep_free)
            t_sqa = p.op("act", lambda E, a=src_ap, s=st_:
                         E.activation(TRASH, a[:, 0:h], AF.Square, accum_out=s[:, 1:2]),
                         deps=[dep_src, t_sum, ln_state.get("sq")] + dep_free)
            t_sq = p.op("act", lambda E, a=src_ap, s=st_:
                        E.activation(TRASH, a[:, h:2 * h], AF.Square, accum_out=s[:, 8:9]),
                        deps=[t_sqa])
            ln_state["sq"] = t_sq
            t_m = p.op("dve", lambda E, s=st_: E.tensor_scalar(s[:, 2:3], s[:, 0:1], 1.0 / width, None, ALU.mult),
                       deps=[t_sum])
            t_nm2 = p.op("dve", lambda E, s=st_: E.tensor_scalar(s[:, 7:8], s[:, 2:3], s[:, 2:3], -1.0,
                                                                 ALU.mult, ALU.mult),
                         deps=[t_m])
            t_ss = p.op("dve", lambda E, s=st_: E.tensor_scalar(s[:, 3:4], s[:, 1:2], s[:, 8:9], None, ALU.add),
                        deps=[t_sq])
            t_vpe = p.op("dve", lambda E, s=st_: E.tensor_scalar(s[:, 4:5], s[:, 3:4], 1.0 / width, s[:, 7:8],
                                                                 ALU.mult, ALU.add),
                         deps=[t_ss, t_nm2])
            t_vpe2 = p.op("dve", lambda E, s=st_: E.tensor_scalar(s[:, 3:4], s[:, 4:5], EPS, None, ALU.add),
                          deps=[t_vpe])
            t_ln = p.op("act", lambda E, s=st_: E.activation(s[:, 4:5], s[:, 3:4], AF.Ln),
                        deps=[t_vpe2, t_sq])
            t_rstd = p.op("act", lambda E, s=st_: E.activation(s[:, 5:6], s[:, 4:5], AF.Exp, scale=-0.5),
                          deps=[t_ln])
            t_nmr = p.op("dve", lambda E, s=st_: E.tensor_scalar(s[:, 6:7], s[:, 2:3], s[:, 5:6], -1.0,
                                                                 ALU.mult, ALU.mult),
                         deps=[t_rstd, t_m])
            return t_sum, t_sq, t_rstd, t_nmr

        p.start_phase("p0")
        p.dma("sp", "c0", rb[:, :], rbase)
        p.dma("sp", "c0", bkb[:, :], bkc)
        p.dma("sp", "c0", bvb[:, :], bvc)
        p.dma("sp", "c0", idb[:, :], identin)
        p.dma("sp", "c0", h128b[:, :], h128)
        p.dma("sp", "c0", h64b[:, :], h64)
        p.dma("sp", "c0", ones64[:, :], onesin)
        t_c0 = p.dma("sp", "c0", bq8[:, :], bqc)
        p.op("dve", lambda E: E.tensor_scalar(bq8[:, :], bq8[:, :], 0.125, None, ALU.mult),
             deps=[t_c0])
        p.flush()

        if "p1" in stages:
            p.start_phase("p1a")
            tst = [BIG[:, i * 2048:(i + 1) * 2048] for i in range(3)]
            ust = [BIG[:, 6144 + i * 2048: 6144 + (i + 1) * 2048] for i in range(3)]
            ev_bank, ev_t, st_u = {}, {}, {}
            for a in range(64):
                sl = a % 3
                t_in = p.dma("pool", f"f1i{sl}", tst[sl], w2t[a * 128:(a + 1) * 128, :],
                             deps=[ev_t.get(sl)])
                evs = []
                for cc in range(4):
                    bk = (a * 4 + cc) % 3
                    t_mm = p.op("pe", lambda E, b_=bk, s=sl, c=cc:
                                E.matmul(PA(b_), h128b[:, :], tst[s][:, c * 512:(c + 1) * 512],
                                         start=True, stop=True),
                                deps=[t_in, ev_bank.get(bk)])
                    t_ev = p.op("dve", lambda E, b_=bk, s=sl, c=cc:
                                E.tensor_copy(ust[s][:, c * 512:(c + 1) * 512], PA(b_)),
                                deps=[t_mm, st_u.get(sl)])
                    ev_bank[bk] = t_ev
                    evs.append(t_ev)
                ev_t[sl] = evs[-1]
                st_u[sl] = p.dma("sp", f"f1o{sl}", ubuf[a * 128:(a + 1) * 128, :],
                                 ust[sl], deps=evs)
            p.flush()

            p.start_phase("p1b")
            u2st = [BIG2[0:64, i * 2048:(i + 1) * 2048] for i in range(4)]
            wst = [BIG2[0:64, 8192 + i * 2048: 8192 + (i + 1) * 2048] for i in range(4)]
            ubuf3 = ubuf.rearrange("(a b) d -> b a d", b=128)
            w2pt3 = w2pt.rearrange("(a b) d -> b a d", b=128)
            ev_bank, ev_t, st_u = {}, {}, {}
            for b_ in range(128):
                sl = b_ % 4
                t_in = p.dma("pool", f"f2i{sl}", u2st[sl], ubuf3[b_, :, :], deps=[ev_t.get(sl)])
                evs = []
                for cc in range(4):
                    bk = (b_ * 4 + cc) % 3
                    t_mm = p.op("pe", lambda E, b2=bk, s=sl, c=cc:
                                E.matmul(PSA[0:64, b2 * 512:(b2 + 1) * 512], h64b[:, :],
                                         u2st[s][:, c * 512:(c + 1) * 512], start=True, stop=True),
                                deps=[t_in, ev_bank.get(bk)])
                    t_ev = p.op("dve", lambda E, b2=bk, s=sl, c=cc:
                                E.tensor_scalar(wst[s][:, c * 512:(c + 1) * 512],
                                                PSA[0:64, b2 * 512:(b2 + 1) * 512],
                                                SCALE_H, None, ALU.mult),
                                deps=[t_mm, st_u.get(sl)])
                    ev_bank[bk] = t_ev
                    evs.append(t_ev)
                ev_t[sl] = evs[-1]
                st_u[sl] = p.dma("sp", f"f2o{sl}", w2pt3[b_, :, :], wst[sl], deps=evs)
            p.flush()


        t_xmid_fin = []
        for b in range(nbatch):
            row0 = b * SEQ

            if "p2" in stages:
                p.start_phase(f"p2_{b}")
                xnT = BIG
                WKV = B3B[:, WKV0:WKV0 + 8192]
                XNST = B3B[:, XNST0:XNST0 + 2048]
                lnst = {}
                tile_done = {}
                xn_tr_ev = {}
                pend2 = {}
                state2 = {"prev_app": None, "k2": 0}

                def p2_stageA(nt):
                    sl = nt % 2
                    freed = tile_done.get(nt - 2, [])
                    t_x = p.dma("sp", f"xl{sl}", xf(sl)[:, :],
                                xin[row0 + nt * 128: row0 + (nt + 1) * 128, :], deps=freed)
                    pend2[nt] = emit_ln_stats(xf(sl)[:, :], sl, t_x, freed, lnst)

                def p2_stageB(nt):
                    sl = nt % 2
                    k2 = state2["k2"]
                    t_sum, t_sq, t_rstd, t_nmr = pend2[nt]
                    t_app = p.op("act", lambda E, s=sl: E.activation(XNST, xf(s)[:, :], AF.Identity,
                                                                     bias=stats(s)[:, 6:7],
                                                                     scale=stats(s)[:, 5:6]),
                                 deps=[t_rstd, t_nmr, state2["prev_app"], xn_tr_ev.get(k2 - 1)])
                    state2["prev_app"] = t_app
                    t_ev = None
                    for d4 in range(4):
                        bk = k2 % 4
                        t_tr = None
                        for di in range(4):
                            dt = d4 * 4 + di
                            t_tr = p.op("pe", lambda E, d=dt, b2=bk, i2=di:
                                        E.matmul(PB(b2)[:, i2 * 128:(i2 + 1) * 128],
                                                 XNST[:, d * 128:(d + 1) * 128],
                                                 idb[:, :], start=True, stop=True),
                                        deps=[t_app, xn_tr_ev.get(k2 - 4)] if di == 0 else [],
                                        inc=(di == 3))
                        t_ev = p.op("dve", lambda E, n=nt, d0=d4 * 4, b2=bk:
                                    E.tensor_copy(
                                        bass.AP(BIG, d0 * 2048 + n * 128,
                                                [[32768, 128], [2048, 4], [1, 128]]),
                                        bass.AP(PSB, b2 * 512, [[2048, 128], [128, 4], [1, 128]])),
                                    deps=[t_tr])
                        xn_tr_ev[k2] = t_ev
                        k2 += 1
                    state2["k2"] = k2
                    tile_done[nt] = [t_ev, t_app, t_sum]

                p2_stageA(0)
                for nt in range(1, NT):
                    p2_stageB(nt - 1)
                    p2_stageA(nt)
                p2_stageB(NT - 1)
                k2 = state2["k2"]
                t_xnT = [xn_tr_ev[k2 - 1], xn_tr_ev[k2 - 2]]

                wk_f = []
                for dt in range(NT):
                    t_ = p.dma("pool", f"wk{dt % 2}", WKV[:, dt * 512:(dt + 1) * 512],
                               wkt[dt * 128:(dt + 1) * 128, :])
                    if dt >= NT - 2:
                        wk_f.append(t_)
                KTv = BIG2[:, KT0:KT0 + 8192]
                kev = {}
                t_kmm_last = None
                for kt in range(4):
                    for c in range(NCH):
                        bk = (kt * NCH + c) % 2
                        t_mm = None
                        for dt in range(NT):
                            t_mm = p.op("pe", lambda E, k=kt, cc=c, d=dt, b2=bk:
                                        E.matmul(PA(b2), WKV[:, d * 512 + k * 128: d * 512 + k * 128 + 128],
                                                 xnT[:, d * 2048 + cc * 512: d * 2048 + (cc + 1) * 512],
                                                 start=(d == 0), stop=(d == NT - 1)),
                                        deps=(wk_f + [kev.get(kt * NCH + c - 2)] + t_xnT) if dt == 0 else [],
                                        inc=(dt == NT - 1))
                        t_kmm_last = t_mm
                        kev[kt * NCH + c] = p.op("dve", lambda E, k=kt, cc=c, b2=bk:
                                                 E.tensor_scalar(KTv[:, k * 2048 + cc * 512: k * 2048 + (cc + 1) * 512],
                                                                 PA(b2), bkb[:, k:k + 1], None, ALU.add),
                                                 deps=[t_mm])
                t_kt_done = kev[15]

                wv_f = []
                for dt in range(NT):
                    t_ = p.dma("pool", f"wk{dt % 2}", WKV[:, dt * 512:(dt + 1) * 512],
                               wvt[dt * 128:(dt + 1) * 128, :], deps=[t_kmm_last])
                    if dt >= NT - 2:
                        wv_f.append(t_)
                t_ones = p.op("dve", lambda E: E.memset(BIG2[:, VO0:VO0 + 8320], 1.0),
                              deps=[t_kt_done])
                vev = {}
                for st in range(NT):
                    bk = st % 2
                    t_mm = None
                    for dt in range(NT):
                        t_mm = p.op("pe", lambda E, s=st, d=dt, b2=bk:
                                    E.matmul(PB(b2), xnT[:, d * 2048 + s * 128: d * 2048 + s * 128 + 128],
                                             WKV[:, d * 512:(d + 1) * 512],
                                             start=(d == 0), stop=(d == NT - 1)),
                                    deps=(wv_f + [t_ones, vev.get(st - 2)]) if dt == 0 else [],
                                    inc=(dt == NT - 1))
                    vev[st] = p.op("dve", lambda E, s=st, b2=bk:
                                   E.tensor_copy(
                                       bass.AP(BIG2, VO0 + s * 520, [[32768, 128], [65, 8], [1, 64]]),
                                       bass.AP(PSB, b2 * 512, [[2048, 128], [64, 8], [1, 64]])),
                                   deps=[t_mm])
                t_vo_done = vev[NT - 1]

                WQ = [B3B[:, WQ0 + i * 2048: WQ0 + (i + 1) * 2048] for i in range(2)]
                QS = [B3B[:, QS0 + i * 512: QS0 + (i + 1) * 512] for i in range(2)]
                q_mm_last = {}
                q_ev = {}
                q_st = {}
                qst_all = []
                for tq in range(NT):
                    sl = tq % 2
                    t_wq = None
                    for dt in range(NT):
                        t_wq = p.dma("pool", f"wq{sl}", WQ[sl][:, dt * 128:(dt + 1) * 128],
                                     wqt[dt * 128:(dt + 1) * 128, tq * 128:(tq + 1) * 128],
                                     deps=[q_mm_last.get(tq - 2)])
                    for c in range(NCH):
                        bk = 2 + (tq * NCH + c) % 2
                        t_mm = None
                        for dt in range(NT):
                            t_mm = p.op("pe", lambda E, s=sl, cc=c, d=dt, b2=bk:
                                        E.matmul(PA(b2), WQ[s][:, d * 128:(d + 1) * 128],
                                                 xnT[:, d * 2048 + cc * 512: d * 2048 + (cc + 1) * 512],
                                                 start=(d == 0), stop=(d == NT - 1)),
                                        deps=[t_wq, q_ev.get(tq * NCH + c - 2)] if dt == 0 else [],
                                        inc=(dt == NT - 1))
                        if c == NCH - 1:
                            q_mm_last[tq] = t_mm
                        ssl = (tq * NCH + c) % 2
                        t_ev = p.op("dve", lambda E, s=ssl, t=tq, b2=bk:
                                    E.tensor_scalar(QS[s][:, :], PA(b2), 0.125,
                                                    bq8[:, t:t + 1], ALU.mult, ALU.add),
                                    deps=[t_mm, q_st.get(tq * NCH + c - 2)])
                        q_ev[tq * NCH + c] = t_ev
                        t_s = p.dma("sp", f"qo{ssl}",
                                    qts[tq * 128:(tq + 1) * 128, c * 512:(c + 1) * 512],
                                    QS[ssl][:, :], deps=[t_ev])
                        q_st[tq * NCH + c] = t_s
                        qst_all.append(t_s)
                p.flush()

            if "p3" in stages:
                p.start_phase(f"p3_{b}")
                if b == 0 and "p1" in stages:
                    for wsrc, wdst, tag in ((w1t, w1tb, "pw1"), (w3t, w3tb, "pw3")):
                        for hb in range(32):
                            for dt in range(NT):
                                p.dma("pool", f"{tag}{(hb * NT + dt) % 3}",
                                      wdst[hb, :, dt * 256:(dt + 1) * 256],
                                      wsrc[dt * 128:(dt + 1) * 128, hb * 256:(hb + 1) * 256])
                attnT = BIG
                MAXST = max(len(stiles_for_chunk(c)) for c in range(NCH))
                NPB = 3
                PBUF = [B3B[:, PB0 + i * MAXST * 512: PB0 + (i + 1) * MAXST * 512]
                        for i in range(NPB)]

                def qtc_ap(qb, tt, r0, r1):
                    if qb == 0:
                        return BIG2[r0:r1, QT0 + tt * 512: QT0 + (tt + 1) * 512]
                    if tt < 13:
                        return B3B[r0:r1, 9728 + tt * 512: 9728 + (tt + 1) * 512]
                    return BIG2[r0:r1, 24704 + (tt - 13) * 512: 24704 + (tt - 12) * 512]
                RBF = [BIG2[0:1, 26240 + i * 512: 26752 + i * 512] for i in range(2)]
                ESTS = [B3F[:, EST0 + i * 1024: EST0 + (i + 1) * 1024] for i in range(4)]
                RCF = B3F[0:1, RCF0:RCF0 + 512]
                BCS = B3F[0:64, BC0:BC0 + 512]
                TMP = B3F[0:64, TMP0:TMP0 + 512]

                qk_war = {}
                stt_war = {}
                pv_war = {}
                mul_war = {}
                bv_war = {}
                rb_war = {}
                bc_war = {}
                bce_war = {}
                qtc_tok = {}
                qtc_reader = {}
                head_meta = {}
                st3 = {"k3": 0}

                def p3_qse(c, j, jg):
                    kp = st3["k3"]
                    sts = stiles_for_chunk(c)
                    h = j // G
                    sg = 64 * (h % 2)
                    slope = SLOPES[j % G]
                    pb = PBUF[jg % NPB]
                    t_exp_last = None
                    idx = 0
                    while idx < len(sts):
                        w = 2 if idx + 1 < len(sts) else 1
                        bank0 = (kp % 2) * 2
                        t_qk = None
                        for q in range(w):
                            st = sts[idx + q]
                            t_qk = p.op("pe", lambda E, s=sg, hh=h, ss=st, b2=bank0 + q, jj=j, qb=c % 2:
                                        E.matmul(PA(b2),
                                                 KTv[s:s + 64, (hh // 2) * 2048 + ss * 128: (hh // 2) * 2048 + (ss + 1) * 128],
                                                 qtc_ap(qb, tau(jj), s, s + 64),
                                                 start=True, stop=True),
                                        deps=[qtc_tok[c], qk_war.get(kp - 2), t_kt_done, t_vo_done]
                                        if q == 0 else [],
                                        inc=(q == w - 1))
                        qtc_reader[c] = t_qk
                        u0 = 1920 + c * 512 - sts[idx] * 128
                        esl = kp % 4
                        t_stt = p.op("dve", lambda E, b0=bank0, u=u0, sp_=slope, es=esl, w_=w:
                                     E.scalar_tensor_tensor(
                                         ESTS[es][:, 0:512 * w_],
                                         bass.AP(rb, u, [[RW, 128], [-128, w_], [1, 512]]),
                                         sp_, PSA[:, b0 * 512: b0 * 512 + 512 * w_],
                                         ALU.mult, ALU.add),
                                     deps=[t_qk, stt_war.get(kp - 4)])
                        qk_war[kp] = t_stt
                        t_exp = p.op("act", lambda E, ii=idx, es=esl, b3=pb, w_=w:
                                     E.activation(b3[:, ii * 512: ii * 512 + 512 * w_],
                                                  ESTS[es][:, 0:512 * w_], AF.Exp),
                                     deps=[t_stt, pv_war.get(jg - NPB)])
                        stt_war[kp] = t_exp
                        t_exp_last = t_exp
                        kp += 1
                        idx += w
                    st3["k3"] = kp
                    head_meta[jg] = (c, j, t_exp_last)

                BCS2 = [B3F[0:64, BC0 + i * 512: BC0 + (i + 1) * 512] for i in range(2)]

                def p3_pvr(jg):
                    c, j, t_exp_last = head_meta[jg]
                    sts = stiles_for_chunk(c)
                    h = j // G
                    pb = PBUF[jg % NPB]
                    j2 = jg % 2
                    t_pv = None
                    for idx, st in enumerate(sts):
                        t_pv = p.op("pe", lambda E, s=st, ii=idx, hh=h, jj2=j2, b3=pb,
                                    first=(idx == 0), last=(idx == len(sts) - 1):
                                    E.matmul(PSB[0:65, jj2 * 512: jj2 * 512 + 512],
                                             BIG2[:, VO0 + s * 520 + 65 * hh: VO0 + s * 520 + 65 * hh + 65],
                                             b3[:, ii * 512:(ii + 1) * 512],
                                             start=first, stop=last),
                                    deps=[t_exp_last, mul_war.get(jg - 2)] if idx == 0 else [],
                                    inc=(idx == len(sts) - 1))
                    pv_war[jg] = t_pv
                    t_ln = p.op("act", lambda E, jj2=j2:
                                E.activation(RCF, PSB[64:65, jj2 * 512: jj2 * 512 + 512], AF.Ln),
                                deps=[t_pv, rb_war.get(jg - 1)])
                    t_expr = p.op("act", lambda E, j2r=jg % 2:
                                  E.activation(RBF[j2r], RCF, AF.Exp, scale=-1.0),
                                  deps=[t_ln, bc_war.get(jg - 2)])
                    rb_war[jg] = t_expr

                def p3_bcb(jg):
                    j2 = jg % 2
                    t_bc = p.op("pe", lambda E, jj2=j2:
                                E.matmul(PSB[0:64, 1024 + jj2 * 512: 1024 + jj2 * 512 + 512],
                                         ones64[:, :], RBF[jj2], start=True, stop=True),
                                deps=[rb_war[jg], bce_war.get(jg - 2)])
                    bc_war[jg] = t_bc
                    t_bce = p.op("dve", lambda E, jj2=j2:
                                 E.tensor_copy(BCS2[jj2], PSB[0:64, 1024 + jj2 * 512: 1024 + jj2 * 512 + 512]),
                                 deps=[t_bc, mul_war.get(jg - 2)])
                    bce_war[jg] = t_bce

                def p3_mulbv(jg):
                    c, j, _ = head_meta[jg]
                    j2 = jg % 2
                    t_mul = p.op("dve", lambda E, jj2=j2:
                                 E.scalar_tensor_tensor(TMP, BCS2[jj2], 1.0,
                                                        PSB[0:64, jj2 * 512: jj2 * 512 + 512],
                                                        ALU.bypass, ALU.mult),
                                 deps=[bce_war[jg], bv_war.get(jg - 1)])
                    mul_war[jg] = t_mul
                    t_bv = p.op("act", lambda E, jj=j, cc=c:
                                E.activation(
                                    attnT[64 * (jj % 2): 64 * (jj % 2) + 64,
                                          (jj // 2) * 2048 + cc * 512: (jj // 2) * 2048 + (cc + 1) * 512],
                                    TMP, AF.Identity,
                                    bias=bvb[64 * (jj % 2): 64 * (jj % 2) + 64, jj // 2: jj // 2 + 1]),
                                deps=[t_mul])
                    bv_war[jg] = t_bv

                jg = 0
                for c in range(NCH):
                    dep_prev = [qtc_reader.get(c - 2)] + qst_all[-2:]
                    t_ld = None
                    for j in range(NH):
                        h = j // G
                        t_ld = p.dma("sp", f"qtc{c % 2}",
                                     qtc_ap(c % 2, tau(j), 64 * (h % 2), 64 * (h % 2) + 64),
                                     qts[(j // 2) * 128 + 64 * (j % 2): (j // 2) * 128 + 64 * (j % 2) + 64,
                                         c * 512:(c + 1) * 512],
                                     deps=dep_prev)
                    qtc_tok[c] = t_ld
                    for j in range(NH):
                        p3_qse(c, j, jg)
                        if jg >= 4:
                            p3_mulbv(jg - 4)
                        if jg >= 2:
                            p3_pvr(jg - 2)
                        if jg >= 3:
                            p3_bcb(jg - 3)
                        jg += 1
                p3_mulbv(jg - 4)
                p3_pvr(jg - 2)
                p3_bcb(jg - 3)
                p3_mulbv(jg - 3)
                p3_pvr(jg - 1)
                p3_bcb(jg - 2)
                p3_mulbv(jg - 2)
                p3_bcb(jg - 1)
                p3_mulbv(jg - 1)
                t_attn_done = [bv_war[jg - 1], bv_war[jg - 2]]
                p.flush()

            if "p4" in stages:
                p.start_phase(f"p4_{b}")
                WOB = BIG2[:, WO0:WO0 + 32768]
                ZN = LNOUT
                wo_f = []
                for dqt in range(NT):
                    t_ = p.dma("pool", f"wo{dqt % 2}", WOB[:, dqt * 2048:(dqt + 1) * 2048],
                               wot[dqt * 128:(dqt + 1) * 128, :])
                    if dqt >= NT - 2:
                        wo_f.append(t_)
                lnst = {}
                tile_done = {}
                st4 = {}
                pend4 = {}
                state4 = {"prev_app": None}
                _n4 = min(NT, P4LIM)

                def p4_stageA(nt):
                    sl = nt % 2
                    psum = PSA if sl == 0 else PSB
                    freed = tile_done.get(nt - 2, [])
                    t_mm = None
                    for dc in range(4):
                        for dqt in range(NT):
                            t_mm = p.op("pe", lambda E, d=dc, q2=dqt, n=nt, ps=psum:
                                        E.matmul(ps[:, d * 512:(d + 1) * 512],
                                                 attnT[:, q2 * 2048 + n * 128: q2 * 2048 + n * 128 + 128],
                                                 WOB[:, q2 * 2048 + d * 512: q2 * 2048 + (d + 1) * 512],
                                                 start=(q2 == 0), stop=(q2 == NT - 1)),
                                        deps=(wo_f + t_attn_done + freed) if (dc == 0 and dqt == 0) else [],
                                        inc=(dqt == NT - 1 and dc == 3))
                    t_x = p.dma("sp", f"xl{sl}", xf(sl)[:, :],
                                xin[row0 + nt * 128: row0 + (nt + 1) * 128, :],
                                deps=freed + [st4.get(nt - 2)])
                    pend4[nt] = (emit_ln_stats(psum[:, :], sl, t_mm, freed, lnst), t_x, psum)

                def p4_stageB(nt):
                    sl = nt % 2
                    (t_sum, t_sq, t_rstd, t_nmr), t_x, psum = pend4[nt]
                    t_app = p.op("act", lambda E, s=sl, ps=psum:
                                 E.activation(ZN, ps[:, :], AF.Identity,
                                              bias=stats(s)[:, 6:7], scale=stats(s)[:, 5:6]),
                                 deps=[t_rstd, t_nmr, state4["prev_app"], st4.get(nt - 1)])
                    state4["prev_app"] = t_app
                    t_add = p.op("dve", lambda E, s=sl: E.tensor_add(ZN, ZN, xf(s)[:, :]),
                                 deps=[t_app, t_x])
                    t_s = p.dma("sp", f"xm{sl}",
                                xmid[row0 + nt * 128: row0 + (nt + 1) * 128, :], ZN,
                                deps=[t_add])
                    st4[nt] = t_s
                    tile_done[nt] = [t_app, t_sum]

                p4_stageA(0)
                for nt in range(1, _n4):
                    p4_stageB(nt - 1)
                    p4_stageA(nt)
                p4_stageB(_n4 - 1)
                if _n4 >= 2:
                    t_xmid_fin += [st4[_n4 - 1], st4[_n4 - 2]]
                p.flush()

        if "p5" in stages:
            p.start_phase("p5")
            NCHT = (nbatch * SEQ) // 512
            N5 = min(NCHT, P5LIM)
            hidT = BIG
            Z2B = BIG2[:, Z20:Z20 + 8192]
            XN2T = BIG2[:, XN20:XN20 + 8192]
            W1B = [BIG2[:, W1B0 + i * 4096: W1B0 + (i + 1) * 4096] for i in range(2)]
            W3B = [BIG2[:, W3B0 + i * 4096: W3B0 + (i + 1) * 4096] for i in range(2)]
            W2PS = [B3B[:, W2PS0 + i * 1024: W2PS0 + (i + 1) * 1024] for i in range(3)]
            XN2S = B3B[:, XN2S0:XN2S0 + 8192]

            lnst = {}
            ln2_app = {}
            ln2_done = {}
            hid_war = {}
            w2_rd = {}
            dwn_ev = {}
            out_sts = {}
            up_last = {}
            pb_war = {}
            ln3_done = {}
            pend5 = {}
            xn2_tok = {}
            st5 = {"prev_app2": None, "prev_app3": None, "prev_add3": None, "k5": 0}

            tr4_done = {}

            def ln2_stats(g):
                for lt in range(4):
                    nt = g * 4 + lt
                    sl = lt % 2
                    freed = ln2_done.get((g, lt - 2), []) + ln2_done.get((g - 1, lt + 2), [])
                    t_x = p.dma("sp", f"x5{sl}", xf(sl)[:, :],
                                xmid[nt * 128:(nt + 1) * 128, :],
                                deps=freed + t_xmid_fin[-2:])
                    t_sum, t_sq, t_rstd, t_nmr = emit_ln_stats(xf(sl)[:, :], sl, t_x, freed, lnst)
                    t_app = p.op("act", lambda E, s=sl, l=lt:
                                 E.activation(XN2S[:, l * 2048:(l + 1) * 2048], xf(s)[:, :],
                                              AF.Identity,
                                              bias=stats(s)[:, 6:7], scale=stats(s)[:, 5:6]),
                                 deps=[t_rstd, t_nmr, st5["prev_app2"],
                                       tr4_done.get((g - 1, lt))])
                    st5["prev_app2"] = t_app
                    ln2_app[(g, lt)] = t_app
                    ln2_done[(g, lt)] = [t_app, t_sum]

            def ln2_apply_tr(g, free_dep, updep):
                k5 = st5["k5"]
                for lt in range(4):
                    t_ev = None
                    for d4 in range(4):
                        bk = k5 % 4
                        t_tr = None
                        for di in range(4):
                            dt = d4 * 4 + di
                            t_tr = p.op("pe", lambda E, d=dt, l=lt, b2=bk, i2=di:
                                        E.matmul(PB(b2)[:, i2 * 128:(i2 + 1) * 128],
                                                 XN2S[:, l * 2048 + d * 128: l * 2048 + (d + 1) * 128],
                                                 idb[:, :], start=True, stop=True),
                                        deps=[ln2_app[(g, lt)], pb_war.get(k5 - 4),
                                              updep, free_dep] if di == 0 else [],
                                        inc=(di == 3))
                        t_ev = p.op("dve", lambda E, d0=d4 * 4, l=lt, b2=bk:
                                    E.tensor_copy(
                                        bass.AP(BIG2, XN20 + d0 * 512 + l * 128,
                                                [[32768, 128], [512, 4], [1, 128]]),
                                        bass.AP(PSB, b2 * 512, [[2048, 128], [128, 4], [1, 128]])),
                                    deps=[t_tr])
                        pb_war[k5] = t_ev
                        k5 += 1
                    tr4_done[(g, lt)] = t_ev
                st5["k5"] = k5
                xn2_tok[g] = [pb_war[k5 - 1], pb_war[k5 - 2]]

            def up(g):
                t_hl = None
                for hb in range(32):
                    wsl = hb % 2
                    wfree = [hid_war.get((g, (hb - 2) * 2 + 1)),
                             hid_war.get((g - 1, 60 + (hb % 2) * 2 + 1)) if hb < 2 else None]
                    t_w1 = p.dma("pool", f"w1s{wsl}", W1B[wsl][:, :], w1tb[hb, :, :], deps=wfree)
                    t_w3 = p.dma("pool", f"w3s{wsl}", W3B[wsl][:, :], w3tb[hb, :, :], deps=wfree)
                    for hl in range(2):
                        ht = hb * 2 + hl
                        bkg = (ht % 2) * 2
                        free_h = [hid_war.get((g, ht - 2)),
                                  dwn_ev.get((g - 1, 1, 7)) if ht < 2 else None]
                        t_g = None
                        t_l = None
                        for dt in range(NT):
                            t_g = p.op("pe", lambda E, s=wsl, d=dt, h2=hl, b2=bkg:
                                       E.matmul(PA(b2), W1B[s][:, d * 256 + h2 * 128: d * 256 + h2 * 128 + 128],
                                                XN2T[:, d * 512:(d + 1) * 512],
                                                start=(d == 0), stop=(d == NT - 1)),
                                       deps=([t_w1] + xn2_tok[g] + free_h) if dt == 0 else [],
                                       inc=(dt == NT - 1))
                            t_l = p.op("pe", lambda E, s=wsl, d=dt, h2=hl, b2=bkg + 1:
                                       E.matmul(PA(b2), W3B[s][:, d * 256 + h2 * 128: d * 256 + h2 * 128 + 128],
                                                XN2T[:, d * 512:(d + 1) * 512],
                                                start=(d == 0), stop=(d == NT - 1)),
                                       deps=([t_w3] + free_h) if dt == 0 else [],
                                       inc=(dt == NT - 1))
                        t_si = p.op("act", lambda E, b2=bkg:
                                    E.activation(SIL1, PA(b2), AF.Silu),
                                    deps=[t_g, hid_war.get((g, ht - 1)), hid_war.get((g, ht - 2))])
                        t_h = p.op("dve", lambda E, b2=bkg + 1, h2=ht:
                                   E.tensor_mul(hidT[:, h2 * 512:(h2 + 1) * 512], SIL1, PA(b2)),
                                   deps=[t_si, t_l, dwn_ev.get((g - 1, 1, 7))])
                        hid_war[(g, ht)] = t_h
                        t_hl = t_l
                up_last[g] = t_hl

            def downpass(g, pi):
                t_hid = hid_war[(g, 63)]
                for ht in range(64):
                    wsl = ht % 3
                    prev_ht = [63, 61, 62][ht] if ht < 3 else ht - 3
                    wprev = w2_rd.get((pi, prev_ht)) if ht >= 3 else w2_rd.get((1 - pi, prev_ht))
                    t_w2 = p.dma("pool", f"w2s{wsl}", W2PS[wsl][:, :],
                                 w2pt[ht * 128:(ht + 1) * 128, pi * 1024:(pi + 1) * 1024],
                                 deps=[wprev])
                    t_rd = None
                    for nt in range(4):
                        for dh in range(2):
                            bank8 = nt * 2 + dh
                            psum = PSA if bank8 < 4 else PSB
                            bk = bank8 % 4
                            dep0 = [t_w2]
                            if ht == 0:
                                dep0 += [t_hid, pb_war.get(st5["k5"] - 1),
                                         dwn_ev.get((g, 0, 7)) if pi == 1 else dwn_ev.get((g - 1, 1, 7))]
                            t_rd = p.op("pe", lambda E, s=wsl, n=nt, d=dh, h2=ht, ps=psum, b2=bk:
                                        E.matmul(ps[:, b2 * 512:(b2 + 1) * 512],
                                                 hidT[:, h2 * 512 + n * 128: h2 * 512 + n * 128 + 128],
                                                 W2PS[s][:, d * 512:(d + 1) * 512],
                                                 start=(h2 == 0), stop=(h2 == 63)),
                                        deps=dep0,
                                        inc=(nt == 3 and dh == 1))
                    w2_rd[(pi, ht)] = t_rd

            def downevict(g, pi):
                for nt in range(4):
                    for dh in range(2):
                        bank8 = nt * 2 + dh
                        psum = PSA if bank8 < 4 else PSB
                        bk = bank8 % 4
                        t_z = p.op("dve", lambda E, ps=psum, b2=bk, n=nt, d=dh, pp=pi:
                                   E.tensor_copy(Z2B[:, n * 2048 + pp * 1024 + d * 512:
                                                     n * 2048 + pp * 1024 + (d + 1) * 512],
                                                 ps[:, b2 * 512:(b2 + 1) * 512]),
                                   deps=[w2_rd[(pi, 63)]] + (ln3_done.get((g - 1, nt)) or []))
                        dwn_ev[(g, pi, nt * 2 + dh)] = t_z

            def ln3(g):
                for lt in range(4):
                    nt = g * 4 + lt
                    sl = 2 + lt % 2
                    zsrc = BIG2[:, Z20 + lt * 2048: Z20 + (lt + 1) * 2048]
                    t_x = p.dma("sp", "x6", XF6,
                                xmid[nt * 128:(nt + 1) * 128, :],
                                deps=[st5["prev_add3"]] + t_xmid_fin[-2:])
                    zdeps = [dwn_ev[(g, 0, lt * 2)], dwn_ev[(g, 0, lt * 2 + 1)],
                             dwn_ev[(g, 1, lt * 2)], dwn_ev[(g, 1, lt * 2 + 1)]]
                    t_sum, t_sq, t_rstd, t_nmr = emit_ln_stats(
                        zsrc, sl, zdeps[0],
                        zdeps[1:] + (ln3_done.get((g, lt - 2)) or []), lnst)
                    t_app = p.op("act", lambda E, s=sl, z=zsrc:
                                 E.activation(LNOUT, z, AF.Identity,
                                              bias=stats(s)[:, 6:7], scale=stats(s)[:, 5:6]),
                                 deps=[t_rstd, t_nmr, st5["prev_app3"],
                                       out_sts.get((g, lt - 1)),
                                       out_sts.get((g - 1, 3)) if lt == 0 else None])
                    st5["prev_app3"] = t_app
                    t_add = p.op("dve", lambda E: E.tensor_add(LNOUT, LNOUT, XF6),
                                 deps=[t_app, t_x])
                    st5["prev_add3"] = t_add
                    t_s = p.dma("sp", f"oo{lt % 2}", out[nt * 128:(nt + 1) * 128, :], LNOUT,
                                deps=[t_add])
                    out_sts[(g, lt)] = t_s
                    ln3_done[(g, lt)] = [t_app, t_sum]

            ln2_stats(0)
            ln2_apply_tr(0, None, None)
            for g_ in range(N5):
                if g_ + 1 < N5:
                    ln2_stats(g_ + 1)
                up(g_)
                if g_ + 1 < N5:
                    ln2_apply_tr(g_ + 1, dwn_ev.get((g_ - 1, 1, 7)), up_last[g_])
                downpass(g_, 0)
                downevict(g_, 0)
                downpass(g_, 1)
                downevict(g_, 1)
                ln3(g_)

            fin = [out_sts[(N5 - 1, lt)] for lt in range(4)]
            p.op("sp", lambda E: E.nop(), deps=fin)
            p.flush()
        else:
            p.start_phase("pcopy")
            prevc = {}
            for nt in range(TOK // 128):
                sl = nt % 2
                tl = p.dma("sp", f"cl{sl}", xf(sl)[:, :],
                           xin[nt * 128:(nt + 1) * 128, :], deps=[prevc.get(nt - 2)])
                ts = p.dma("sp", f"co{sl}", out[nt * 128:(nt + 1) * 128, :],
                           xf(sl)[:, :], deps=[tl])
                prevc[nt] = ts
            p.op("sp", lambda E: E.nop(),
                 deps=[prevc[TOK // 128 - 1], prevc[TOK // 128 - 2]])
            p.flush()

    return nc


def _hadamard(n):
    H = np.array([[1.0]], dtype=np.float64)
    while H.shape[0] < n:
        H = np.block([[H, H], [H, -H]])
    return H


def make_inputs(x, Wq, bq, Wk, bk, Wv, bv, Wo, W1, W2, W3, ncore=8):
    f32 = np.float32
    bf16 = ml_dtypes.bfloat16
    x = np.ascontiguousarray(np.asarray(x, dtype=f32))
    B = x.shape[0]
    bq = np.asarray(bq, f32)
    bk = np.asarray(bk, f32)
    bv = np.asarray(bv, f32)
    bvc = np.zeros((128, 16), f32)
    for jt in range(16):
        bvc[0:64, jt] = bv[64 * ((2 * jt) // G): 64 * ((2 * jt) // G) + 64]
        bvc[64:128, jt] = bv[64 * ((2 * jt + 1) // G): 64 * ((2 * jt + 1) // G) + 64]
    pcol = np.arange(128)[:, None]
    u = np.arange(RW)[None, :]
    shared = dict(
        wqt=np.ascontiguousarray(np.asarray(Wq, f32).T),
        wkt=np.ascontiguousarray(np.asarray(Wk, f32).T),
        wvt=np.ascontiguousarray(np.asarray(Wv, f32).T),
        wot=np.ascontiguousarray(np.asarray(Wo, f32).T),
        w1t=np.ascontiguousarray(np.asarray(W1, f32).T),
        w3t=np.ascontiguousarray(np.asarray(W3, f32).T),
        w2t=np.ascontiguousarray(np.asarray(W2, f32).T),
        bqc=np.ascontiguousarray(bq.reshape(16, 128).T),
        bkc=np.ascontiguousarray(bk.reshape(4, 128).T),
        bvc=bvc,
        rbase=(-np.abs(u - 1920 - pcol)).astype(bf16),
        h128=_hadamard(128).astype(bf16),
        h64=_hadamard(64).astype(bf16),
        identin=np.eye(128, dtype=bf16),
        onesin=np.ones((1, 64), dtype=bf16),
    )
    bpc = B // ncore
    in_maps = []
    for c in range(ncore):
        m = dict(shared)
        m["xin"] = np.ascontiguousarray(x[c * bpc:(c + 1) * bpc].reshape(bpc * SEQ, DM))
        in_maps.append(m)
    return in_maps, bpc


def kernel(L0_attn_hidden_states, L0_attn_Wq, L0_attn_bq, L0_attn_Wk, L0_attn_bk,
           L0_attn_Wv, L0_attn_bv, L0_attn_Wo, L0_ffn_W1, L0_ffn_W2, L0_ffn_W3):
    in_maps, bpc = make_inputs(L0_attn_hidden_states, L0_attn_Wq, L0_attn_bq,
                               L0_attn_Wk, L0_attn_bk, L0_attn_Wv, L0_attn_bv,
                               L0_attn_Wo, L0_ffn_W1, L0_ffn_W2, L0_ffn_W3)
    nc = build(bpc)
    res = run_bass_kernel_spmd(nc, in_maps, core_ids=list(range(8)))
    outs = [r["out"].reshape(bpc, SEQ, DM) for r in res.results]
    return np.concatenate(outs, axis=0)


# version 12 (final)
# speedup vs baseline: 1.1623x; 1.1623x over previous
import math
from contextlib import ExitStack

import numpy as np
import ml_dtypes

import concourse.bass as bass
import concourse.mybir as mybir
from concourse.bass_utils import run_bass_kernel_spmd

F32 = mybir.dt.float32
BF16 = mybir.dt.bfloat16
AF = mybir.ActivationFunctionType
ALU = mybir.AluOpType
AX = mybir.AxisListType

SEQ = 2048
DM = 2048
DKV = 512
HID = 8192
NH = 32
KVH = 8
G = NH // KVH
NT = SEQ // 128
NCH = SEQ // 512
RW = 3968
EPS = 1e-5
SCALE_H = float(np.float32(1.0 / np.sqrt(8192.0)))
SLOPES = [float(np.float32(2.0 ** (-gi / G))) for gi in range(G)]
NB = 2
FULL_ATTN = False
BAND_PAD = 1
P5LIM = 99
P4LIM = 99

ENGINES = ("pe", "act", "dve", "pool", "sp")
BLKNAME = {"pe": "tensor", "act": "scalar", "dve": "vector", "pool": "gpsimd", "sp": "sync"}

KT0, VO0, QT0 = 0, 8192, 16512
WO0 = 0
Z20, XN20, W1B0, W3B0 = 0, 8192, 16384, 24576
WKV0, WQ0, QS0, XNST0 = 0, 8192, 12288, 13312
PB0, QTB0, RB16 = 0, 6144, 14336
XN2S0 = 3072
W2PS0 = 0
XF0 = 0
XF60 = 4096
STATS0 = 6144
TRASH0 = 6208
SIL0 = 7232
LNOUT0 = 7744
B3F_COLS = 7744 + 2048
EST0, RCF0, BC0, TMP0 = 0, 9280, 7744, 8768


def stiles_for_chunk(c):
    if FULL_ATTN:
        return list(range(NT))
    lo = max(0, 4 * c - BAND_PAD)
    hi = min(NT, 4 * c + 4 + BAND_PAD)
    return list(range(lo, hi))


def tau(j):
    return ((j // G) // 2) * 4 + (j % G)


class Prog:
    def __init__(self, nc, ctx):
        self.nc = nc
        self.ctx = ctx
        self.sems = {}
        self.step = {}
        self.cnt = {}
        self.waited = {}
        self.streams = {e: [] for e in ENGINES}
        self.eq = {}
        self.bar = ()

    def queue(self, name, step=1):
        if name not in self.sems:
            self.sems[name] = self.ctx.enter_context(self.nc.semaphore(name))
            self.step[name] = step
            self.cnt[name] = 0
        return name

    def start_phase(self, tag):
        for e in ENGINES:
            self.eq[e] = self.queue(f"eng.{e}")
        self.bar = tuple((q, v) for q, v in self.cnt.items() if v > 0)

    def op(self, eng, make, deps=(), q=None, inc=True):
        deps = tuple(d for d in deps if d is not None) + self.bar
        q = q if q is not None else self.eq[eng]
        tok = None
        if inc:
            self.cnt[q] += self.step[q]
            tok = (q, self.cnt[q])
        self.streams[eng].append((make, deps, q if inc else None))
        return tok

    def dma(self, eng, slot, out_ap, in_ap, deps=()):
        return self.op(eng, lambda E, o=out_ap, i=in_ap: E.dma_start(o, i),
                       deps=deps, q=self.queue(slot, 16))

    def flush(self):
        with self.nc.Block() as block:
            for ename in ENGINES:
                if not self.streams[ename]:
                    continue

                def runner(E, ename=ename, items=self.streams[ename]):
                    for make, deps, q in items:
                        best = {}
                        for dq, dv in deps:
                            if best.get(dq, 0) < dv:
                                best[dq] = dv
                        for dq, dv in best.items():
                            key = (ename, dq)
                            if self.waited.get(key, 0) < dv:
                                E.wait_ge(self.sems[dq], dv)
                                self.waited[key] = dv
                        instr = make(E)
                        if q is not None:
                            instr.then_inc(self.sems[q], self.step[q])

                getattr(block, BLKNAME[ename])(runner)
        self.streams = {e: [] for e in ENGINES}


def build(nbatch=NB, stages=("p1", "p2", "p3", "p4", "p5")):
    nc = bass.Bass("TRN2", target_bir_lowering=False)
    TOK = nbatch * SEQ

    xin = nc.dram_tensor("xin", [TOK, DM], F32, kind="ExternalInput").ap()
    wqt = nc.dram_tensor("wqt", [DM, DM], F32, kind="ExternalInput").ap()
    wkt = nc.dram_tensor("wkt", [DM, DKV], F32, kind="ExternalInput").ap()
    wvt = nc.dram_tensor("wvt", [DM, DKV], F32, kind="ExternalInput").ap()
    wot = nc.dram_tensor("wot", [DM, DM], F32, kind="ExternalInput").ap()
    w1t = nc.dram_tensor("w1t", [DM, HID], F32, kind="ExternalInput").ap()
    w3t = nc.dram_tensor("w3t", [DM, HID], F32, kind="ExternalInput").ap()
    w2t = nc.dram_tensor("w2t", [HID, DM], F32, kind="ExternalInput").ap()
    bqc = nc.dram_tensor("bqc", [128, 16], F32, kind="ExternalInput").ap()
    bkc = nc.dram_tensor("bkc", [128, 4], F32, kind="ExternalInput").ap()
    bvc = nc.dram_tensor("bvc", [128, 16], F32, kind="ExternalInput").ap()
    rbase = nc.dram_tensor("rbase", [128, RW], BF16, kind="ExternalInput").ap()
    h128 = nc.dram_tensor("h128", [128, 128], BF16, kind="ExternalInput").ap()
    h64 = nc.dram_tensor("h64", [64, 64], BF16, kind="ExternalInput").ap()
    identin = nc.dram_tensor("identin", [128, 128], BF16, kind="ExternalInput").ap()
    onesin = nc.dram_tensor("onesin", [1, 64], BF16, kind="ExternalInput").ap()

    out = nc.dram_tensor("out", [TOK, DM], F32, kind="ExternalOutput").ap()

    qts = nc.dram_tensor("qts", [DM, SEQ], BF16, kind="Internal").ap()
    xmid = nc.dram_tensor("xmid", [TOK, DM], F32, kind="Internal").ap()
    ubuf = nc.dram_tensor("ubuf", [HID, DM], BF16, kind="Internal").ap()
    w2pt = nc.dram_tensor("w2pt", [HID, DM], BF16, kind="Internal").ap()
    w1tb = nc.dram_tensor("w1tb", [32, 128, 4096], BF16, kind="Internal").ap()
    w3tb = nc.dram_tensor("w3tb", [32, 128, 4096], BF16, kind="Internal").ap()

    with ExitStack() as ctx:
        e = ctx.enter_context
        p = Prog(nc, ctx)

        rb = e(nc.sbuf_tensor("rb", [128, RW], BF16))
        bq8 = e(nc.sbuf_tensor("bq8", [128, 16], F32))
        bkb = e(nc.sbuf_tensor("bkb", [128, 4], F32))
        bvb = e(nc.sbuf_tensor("bvb", [128, 16], F32))
        idb = e(nc.sbuf_tensor("idb", [128, 128], BF16))
        h128b = e(nc.sbuf_tensor("h128b", [128, 128], BF16))
        h64b = e(nc.sbuf_tensor("h64b", [64, 64], BF16))
        ones64 = e(nc.sbuf_tensor("ones64", [1, 64], BF16))
        BIG = e(nc.sbuf_tensor("BIG", [128, 32768], BF16))
        BIG2 = e(nc.sbuf_tensor("BIG2", [128, 32768], BF16))
        B3B = e(nc.sbuf_tensor("B3B", [128, 16384], BF16))
        B3F = e(nc.sbuf_tensor("B3F", [128, B3F_COLS], F32))
        PSA = e(nc.psum_tensor("PSA", [128, 2048], F32))
        PSB = e(nc.psum_tensor("PSB", [128, 2048], F32))

        def PA(bk):
            return PSA[:, bk * 512:(bk + 1) * 512]

        def PB(bk):
            return PSB[:, bk * 512:(bk + 1) * 512]

        def xf(i):
            return B3F[:, XF0 + i * 2048: XF0 + (i + 1) * 2048]

        XF6 = B3F[:, XF60:XF60 + 2048]

        def stats(i):
            return B3F[:, STATS0 + i * 12: STATS0 + (i + 1) * 12]

        TRASH = B3F[:, TRASH0:TRASH0 + 1024]
        SIL1 = B3F[:, SIL0:SIL0 + 512]
        LNOUT = B3F[:, LNOUT0:LNOUT0 + 2048]

        def emit_ln_stats(src_ap, sl, dep_src, dep_free, ln_state, width=2048):
            st_ = stats(sl)
            h = width // 2
            t_sum = p.op("dve", lambda E, a=src_ap, s=st_: E.reduce_sum(s[:, 0:1], a, axis=AX.X),
                         deps=[dep_src] + dep_free)
            t_sqa = p.op("act", lambda E, a=src_ap, s=st_:
                         E.activation(TRASH, a[:, 0:h], AF.Square, accum_out=s[:, 1:2]),
                         deps=[dep_src, t_sum, ln_state.get("sq")] + dep_free)
            t_sq = p.op("act", lambda E, a=src_ap, s=st_:
                        E.activation(TRASH, a[:, h:2 * h], AF.Square, accum_out=s[:, 8:9]),
                        deps=[t_sqa])
            ln_state["sq"] = t_sq
            t_m = p.op("dve", lambda E, s=st_: E.tensor_scalar(s[:, 2:3], s[:, 0:1], 1.0 / width, None, ALU.mult),
                       deps=[t_sum])
            t_nm2 = p.op("dve", lambda E, s=st_: E.tensor_scalar(s[:, 7:8], s[:, 2:3], s[:, 2:3], -1.0,
                                                                 ALU.mult, ALU.mult),
                         deps=[t_m])
            t_ss = p.op("dve", lambda E, s=st_: E.tensor_scalar(s[:, 3:4], s[:, 1:2], s[:, 8:9], None, ALU.add),
                        deps=[t_sq])
            t_vpe = p.op("dve", lambda E, s=st_: E.tensor_scalar(s[:, 4:5], s[:, 3:4], 1.0 / width, s[:, 7:8],
                                                                 ALU.mult, ALU.add),
                         deps=[t_ss, t_nm2])
            t_vpe2 = p.op("dve", lambda E, s=st_: E.tensor_scalar(s[:, 3:4], s[:, 4:5], EPS, None, ALU.add),
                          deps=[t_vpe])
            t_ln = p.op("act", lambda E, s=st_: E.activation(s[:, 4:5], s[:, 3:4], AF.Ln),
                        deps=[t_vpe2, t_sq])
            t_rstd = p.op("act", lambda E, s=st_: E.activation(s[:, 5:6], s[:, 4:5], AF.Exp, scale=-0.5),
                          deps=[t_ln])
            t_nmr = p.op("dve", lambda E, s=st_: E.tensor_scalar(s[:, 6:7], s[:, 2:3], s[:, 5:6], -1.0,
                                                                 ALU.mult, ALU.mult),
                         deps=[t_rstd, t_m])
            return t_sum, t_sq, t_rstd, t_nmr

        p.start_phase("p0")
        p.dma("sp", "c0", rb[:, :], rbase)
        p.dma("sp", "c0", bkb[:, :], bkc)
        p.dma("sp", "c0", bvb[:, :], bvc)
        p.dma("sp", "c0", idb[:, :], identin)
        p.dma("sp", "c0", h128b[:, :], h128)
        p.dma("sp", "c0", h64b[:, :], h64)
        p.dma("sp", "c0", ones64[:, :], onesin)
        t_c0 = p.dma("sp", "c0", bq8[:, :], bqc)
        p.op("dve", lambda E: E.tensor_scalar(bq8[:, :], bq8[:, :], 0.125, None, ALU.mult),
             deps=[t_c0])
        p.flush()

        if "p1" in stages:
            p.start_phase("p1a")
            tst = [BIG[:, i * 2048:(i + 1) * 2048] for i in range(3)]
            ust = [BIG[:, 6144 + i * 2048: 6144 + (i + 1) * 2048] for i in range(3)]
            ev_bank, ev_t, st_u = {}, {}, {}
            for a in range(64):
                sl = a % 3
                t_in = p.dma("pool", f"f1i{sl}", tst[sl], w2t[a * 128:(a + 1) * 128, :],
                             deps=[ev_t.get(sl)])
                evs = []
                for cc in range(4):
                    bk = (a * 4 + cc) % 3
                    t_mm = p.op("pe", lambda E, b_=bk, s=sl, c=cc:
                                E.matmul(PA(b_), h128b[:, :], tst[s][:, c * 512:(c + 1) * 512],
                                         start=True, stop=True),
                                deps=[t_in, ev_bank.get(bk)])
                    t_ev = p.op("dve", lambda E, b_=bk, s=sl, c=cc:
                                E.tensor_copy(ust[s][:, c * 512:(c + 1) * 512], PA(b_)),
                                deps=[t_mm, st_u.get(sl)])
                    ev_bank[bk] = t_ev
                    evs.append(t_ev)
                ev_t[sl] = evs[-1]
                st_u[sl] = p.dma("sp", f"f1o{sl}", ubuf[a * 128:(a + 1) * 128, :],
                                 ust[sl], deps=evs)
            p.flush()

            p.start_phase("p1b")
            u2st = [BIG2[0:64, i * 2048:(i + 1) * 2048] for i in range(4)]
            wst = [BIG2[0:64, 8192 + i * 2048: 8192 + (i + 1) * 2048] for i in range(4)]
            ubuf3 = ubuf.rearrange("(a b) d -> b a d", b=128)
            w2pt3 = w2pt.rearrange("(a b) d -> b a d", b=128)
            ev_bank, ev_t, st_u = {}, {}, {}
            for b_ in range(128):
                sl = b_ % 4
                t_in = p.dma("pool", f"f2i{sl}", u2st[sl], ubuf3[b_, :, :], deps=[ev_t.get(sl)])
                evs = []
                for cc in range(4):
                    bk = (b_ * 4 + cc) % 3
                    t_mm = p.op("pe", lambda E, b2=bk, s=sl, c=cc:
                                E.matmul(PSA[0:64, b2 * 512:(b2 + 1) * 512], h64b[:, :],
                                         u2st[s][:, c * 512:(c + 1) * 512], start=True, stop=True),
                                deps=[t_in, ev_bank.get(bk)])
                    t_ev = p.op("dve", lambda E, b2=bk, s=sl, c=cc:
                                E.tensor_scalar(wst[s][:, c * 512:(c + 1) * 512],
                                                PSA[0:64, b2 * 512:(b2 + 1) * 512],
                                                SCALE_H, None, ALU.mult),
                                deps=[t_mm, st_u.get(sl)])
                    ev_bank[bk] = t_ev
                    evs.append(t_ev)
                ev_t[sl] = evs[-1]
                st_u[sl] = p.dma("sp", f"f2o{sl}", w2pt3[b_, :, :], wst[sl], deps=evs)
            p.flush()


        t_xmid_fin = []
        for b in range(nbatch):
            row0 = b * SEQ

            if "p2" in stages:
                p.start_phase(f"p2_{b}")
                xnT = BIG
                WKV = B3B[:, WKV0:WKV0 + 8192]
                XNST = B3B[:, XNST0:XNST0 + 2048]
                lnst = {}
                tile_done = {}
                xn_tr_ev = {}
                pend2 = {}
                state2 = {"prev_app": None, "k2": 0}

                def p2_stageA(nt):
                    sl = nt % 2
                    freed = tile_done.get(nt - 2, [])
                    t_x = p.dma("sp", f"xl{sl}", xf(sl)[:, :],
                                xin[row0 + nt * 128: row0 + (nt + 1) * 128, :], deps=freed)
                    pend2[nt] = emit_ln_stats(xf(sl)[:, :], sl, t_x, freed, lnst)

                def p2_stageB(nt):
                    sl = nt % 2
                    k2 = state2["k2"]
                    t_sum, t_sq, t_rstd, t_nmr = pend2[nt]
                    t_app = p.op("act", lambda E, s=sl: E.activation(XNST, xf(s)[:, :], AF.Identity,
                                                                     bias=stats(s)[:, 6:7],
                                                                     scale=stats(s)[:, 5:6]),
                                 deps=[t_rstd, t_nmr, state2["prev_app"], xn_tr_ev.get(k2 - 1)])
                    state2["prev_app"] = t_app
                    t_ev = None
                    for d4 in range(4):
                        bk = k2 % 4
                        t_tr = None
                        for di in range(4):
                            dt = d4 * 4 + di
                            t_tr = p.op("pe", lambda E, d=dt, b2=bk, i2=di:
                                        E.matmul(PB(b2)[:, i2 * 128:(i2 + 1) * 128],
                                                 XNST[:, d * 128:(d + 1) * 128],
                                                 idb[:, :], start=True, stop=True),
                                        deps=[t_app, xn_tr_ev.get(k2 - 4)] if di == 0 else [],
                                        inc=(di == 3))
                        t_ev = p.op("dve", lambda E, n=nt, d0=d4 * 4, b2=bk:
                                    E.tensor_copy(
                                        bass.AP(BIG, d0 * 2048 + n * 128,
                                                [[32768, 128], [2048, 4], [1, 128]]),
                                        bass.AP(PSB, b2 * 512, [[2048, 128], [128, 4], [1, 128]])),
                                    deps=[t_tr])
                        xn_tr_ev[k2] = t_ev
                        k2 += 1
                    state2["k2"] = k2
                    tile_done[nt] = [t_ev, t_app, t_sum]

                p2_stageA(0)
                for nt in range(1, NT):
                    p2_stageB(nt - 1)
                    p2_stageA(nt)
                p2_stageB(NT - 1)
                k2 = state2["k2"]
                t_xnT = [xn_tr_ev[k2 - 1], xn_tr_ev[k2 - 2]]

                wk_f = []
                for dt in range(NT):
                    t_ = p.dma("pool", f"wk{dt % 2}", WKV[:, dt * 512:(dt + 1) * 512],
                               wkt[dt * 128:(dt + 1) * 128, :])
                    if dt >= NT - 2:
                        wk_f.append(t_)
                KTv = BIG2[:, KT0:KT0 + 8192]
                kev = {}
                t_kmm_last = None
                for kt in range(4):
                    for c in range(NCH):
                        bk = (kt * NCH + c) % 2
                        t_mm = None
                        for dt in range(NT):
                            t_mm = p.op("pe", lambda E, k=kt, cc=c, d=dt, b2=bk:
                                        E.matmul(PA(b2), WKV[:, d * 512 + k * 128: d * 512 + k * 128 + 128],
                                                 xnT[:, d * 2048 + cc * 512: d * 2048 + (cc + 1) * 512],
                                                 start=(d == 0), stop=(d == NT - 1)),
                                        deps=(wk_f + [kev.get(kt * NCH + c - 2)] + t_xnT) if dt == 0 else [],
                                        inc=(dt == NT - 1))
                        t_kmm_last = t_mm
                        kev[kt * NCH + c] = p.op("dve", lambda E, k=kt, cc=c, b2=bk:
                                                 E.tensor_scalar(KTv[:, k * 2048 + cc * 512: k * 2048 + (cc + 1) * 512],
                                                                 PA(b2), bkb[:, k:k + 1], None, ALU.add),
                                                 deps=[t_mm])
                t_kt_done = kev[15]

                wv_f = []
                for dt in range(NT):
                    t_ = p.dma("pool", f"wk{dt % 2}", WKV[:, dt * 512:(dt + 1) * 512],
                               wvt[dt * 128:(dt + 1) * 128, :], deps=[t_kmm_last])
                    if dt >= NT - 2:
                        wv_f.append(t_)
                t_ones = p.op("dve", lambda E: E.memset(BIG2[:, VO0:VO0 + 8320], 1.0),
                              deps=[t_kt_done])
                vev = {}
                for st in range(NT):
                    bk = st % 2
                    t_mm = None
                    for dt in range(NT):
                        t_mm = p.op("pe", lambda E, s=st, d=dt, b2=bk:
                                    E.matmul(PB(b2), xnT[:, d * 2048 + s * 128: d * 2048 + s * 128 + 128],
                                             WKV[:, d * 512:(d + 1) * 512],
                                             start=(d == 0), stop=(d == NT - 1)),
                                    deps=(wv_f + [t_ones, vev.get(st - 2)]) if dt == 0 else [],
                                    inc=(dt == NT - 1))
                    vev[st] = p.op("dve", lambda E, s=st, b2=bk:
                                   E.tensor_copy(
                                       bass.AP(BIG2, VO0 + s * 520, [[32768, 128], [65, 8], [1, 64]]),
                                       bass.AP(PSB, b2 * 512, [[2048, 128], [64, 8], [1, 64]])),
                                   deps=[t_mm])
                t_vo_done = vev[NT - 1]

                WQ = [B3B[:, WQ0 + i * 2048: WQ0 + (i + 1) * 2048] for i in range(2)]
                QS = [B3B[:, QS0 + i * 512: QS0 + (i + 1) * 512] for i in range(2)]
                q_mm_last = {}
                q_ev = {}
                q_st = {}
                qst_all = []
                for tq in range(NT):
                    sl = tq % 2
                    t_wq = None
                    for dt in range(NT):
                        t_wq = p.dma("pool", f"wq{sl}", WQ[sl][:, dt * 128:(dt + 1) * 128],
                                     wqt[dt * 128:(dt + 1) * 128, tq * 128:(tq + 1) * 128],
                                     deps=[q_mm_last.get(tq - 2)])
                    for c in range(NCH):
                        bk = 2 + (tq * NCH + c) % 2
                        t_mm = None
                        for dt in range(NT):
                            t_mm = p.op("pe", lambda E, s=sl, cc=c, d=dt, b2=bk:
                                        E.matmul(PA(b2), WQ[s][:, d * 128:(d + 1) * 128],
                                                 xnT[:, d * 2048 + cc * 512: d * 2048 + (cc + 1) * 512],
                                                 start=(d == 0), stop=(d == NT - 1)),
                                        deps=[t_wq, q_ev.get(tq * NCH + c - 2)] if dt == 0 else [],
                                        inc=(dt == NT - 1))
                        if c == NCH - 1:
                            q_mm_last[tq] = t_mm
                        ssl = (tq * NCH + c) % 2
                        t_ev = p.op("dve", lambda E, s=ssl, t=tq, b2=bk:
                                    E.tensor_scalar(QS[s][:, :], PA(b2), 0.125,
                                                    bq8[:, t:t + 1], ALU.mult, ALU.add),
                                    deps=[t_mm, q_st.get(tq * NCH + c - 2)])
                        q_ev[tq * NCH + c] = t_ev
                        t_s = p.dma("sp", f"qo{ssl}",
                                    qts[tq * 128:(tq + 1) * 128, c * 512:(c + 1) * 512],
                                    QS[ssl][:, :], deps=[t_ev])
                        q_st[tq * NCH + c] = t_s
                        qst_all.append(t_s)
                p.flush()

            if "p3" in stages:
                p.start_phase(f"p3_{b}")
                if b == 0 and "p1" in stages:
                    for wsrc, wdst, tag in ((w1t, w1tb, "pw1"), (w3t, w3tb, "pw3")):
                        for hb in range(32):
                            for dt in range(NT):
                                p.dma("pool", f"{tag}{(hb * NT + dt) % 3}",
                                      wdst[hb, :, dt * 256:(dt + 1) * 256],
                                      wsrc[dt * 128:(dt + 1) * 128, hb * 256:(hb + 1) * 256])
                attnT = BIG
                MAXST = max(len(stiles_for_chunk(c)) for c in range(NCH))
                NPB = 3
                PBUF = [B3B[:, PB0 + i * MAXST * 512: PB0 + (i + 1) * MAXST * 512]
                        for i in range(NPB)]

                def qtc_ap(qb, tt, r0, r1):
                    if qb == 0:
                        return BIG2[r0:r1, QT0 + tt * 512: QT0 + (tt + 1) * 512]
                    if tt < 13:
                        return B3B[r0:r1, 9728 + tt * 512: 9728 + (tt + 1) * 512]
                    return BIG2[r0:r1, 24704 + (tt - 13) * 512: 24704 + (tt - 12) * 512]
                RBF = [BIG2[0:1, 26240 + i * 512: 26752 + i * 512] for i in range(2)]
                ESTS = [B3F[:, EST0 + i * 1024: EST0 + (i + 1) * 1024] for i in range(4)]
                RCF = B3F[0:1, RCF0:RCF0 + 512]
                BCS = B3F[0:64, BC0:BC0 + 512]
                TMP = B3F[0:64, TMP0:TMP0 + 512]

                qk_war = {}
                stt_war = {}
                pv_war = {}
                mul_war = {}
                bv_war = {}
                rb_war = {}
                bc_war = {}
                bce_war = {}
                qtc_tok = {}
                qtc_reader = {}
                head_meta = {}
                st3 = {"k3": 0}

                def p3_qse(c, j, jg):
                    kp = st3["k3"]
                    sts = stiles_for_chunk(c)
                    h = j // G
                    sg = 64 * (h % 2)
                    slope = SLOPES[j % G]
                    pb = PBUF[jg % NPB]
                    t_exp_last = None
                    idx = 0
                    while idx < len(sts):
                        w = 2 if idx + 1 < len(sts) else 1
                        bank0 = (kp % 2) * 2
                        t_qk = None
                        for q in range(w):
                            st = sts[idx + q]
                            t_qk = p.op("pe", lambda E, s=sg, hh=h, ss=st, b2=bank0 + q, jj=j, qb=c % 2:
                                        E.matmul(PA(b2),
                                                 KTv[s:s + 64, (hh // 2) * 2048 + ss * 128: (hh // 2) * 2048 + (ss + 1) * 128],
                                                 qtc_ap(qb, tau(jj), s, s + 64),
                                                 start=True, stop=True),
                                        deps=[qtc_tok[c], qk_war.get(kp - 2), t_kt_done, t_vo_done]
                                        if q == 0 else [],
                                        inc=(q == w - 1))
                        qtc_reader[c] = t_qk
                        u0 = 1920 + c * 512 - sts[idx] * 128
                        esl = kp % 4
                        t_stt = p.op("dve", lambda E, b0=bank0, u=u0, sp_=slope, es=esl, w_=w:
                                     E.scalar_tensor_tensor(
                                         ESTS[es][:, 0:512 * w_],
                                         bass.AP(rb, u, [[RW, 128], [-128, w_], [1, 512]]),
                                         sp_, PSA[:, b0 * 512: b0 * 512 + 512 * w_],
                                         ALU.mult, ALU.add),
                                     deps=[t_qk, stt_war.get(kp - 4)])
                        qk_war[kp] = t_stt
                        t_exp = p.op("act", lambda E, ii=idx, es=esl, b3=pb, w_=w:
                                     E.activation(b3[:, ii * 512: ii * 512 + 512 * w_],
                                                  ESTS[es][:, 0:512 * w_], AF.Exp),
                                     deps=[t_stt, pv_war.get(jg - NPB)])
                        stt_war[kp] = t_exp
                        t_exp_last = t_exp
                        kp += 1
                        idx += w
                    st3["k3"] = kp
                    head_meta[jg] = (c, j, t_exp_last)

                BCS2 = [B3F[0:64, BC0 + i * 512: BC0 + (i + 1) * 512] for i in range(2)]

                def p3_pvr(jg):
                    c, j, t_exp_last = head_meta[jg]
                    sts = stiles_for_chunk(c)
                    h = j // G
                    pb = PBUF[jg % NPB]
                    j2 = jg % 2
                    t_pv = None
                    for idx, st in enumerate(sts):
                        t_pv = p.op("pe", lambda E, s=st, ii=idx, hh=h, jj2=j2, b3=pb,
                                    first=(idx == 0), last=(idx == len(sts) - 1):
                                    E.matmul(PSB[0:65, jj2 * 512: jj2 * 512 + 512],
                                             BIG2[:, VO0 + s * 520 + 65 * hh: VO0 + s * 520 + 65 * hh + 65],
                                             b3[:, ii * 512:(ii + 1) * 512],
                                             start=first, stop=last),
                                    deps=[t_exp_last, mul_war.get(jg - 2)] if idx == 0 else [],
                                    inc=(idx == len(sts) - 1))
                    pv_war[jg] = t_pv
                    t_ln = p.op("act", lambda E, jj2=j2:
                                E.activation(RCF, PSB[64:65, jj2 * 512: jj2 * 512 + 512], AF.Ln),
                                deps=[t_pv, rb_war.get(jg - 1)])
                    t_expr = p.op("act", lambda E, j2r=jg % 2:
                                  E.activation(RBF[j2r], RCF, AF.Exp, scale=-1.0),
                                  deps=[t_ln, bc_war.get(jg - 2)])
                    rb_war[jg] = t_expr

                def p3_bcb(jg):
                    j2 = jg % 2
                    t_bc = p.op("pe", lambda E, jj2=j2:
                                E.matmul(PSB[0:64, 1024 + jj2 * 512: 1024 + jj2 * 512 + 512],
                                         ones64[:, :], RBF[jj2], start=True, stop=True),
                                deps=[rb_war[jg], bce_war.get(jg - 2)])
                    bc_war[jg] = t_bc
                    t_bce = p.op("dve", lambda E, jj2=j2:
                                 E.tensor_copy(BCS2[jj2], PSB[0:64, 1024 + jj2 * 512: 1024 + jj2 * 512 + 512]),
                                 deps=[t_bc, mul_war.get(jg - 2)])
                    bce_war[jg] = t_bce

                def p3_mulbv(jg):
                    c, j, _ = head_meta[jg]
                    j2 = jg % 2
                    t_mul = p.op("dve", lambda E, jj2=j2:
                                 E.scalar_tensor_tensor(TMP, BCS2[jj2], 1.0,
                                                        PSB[0:64, jj2 * 512: jj2 * 512 + 512],
                                                        ALU.bypass, ALU.mult),
                                 deps=[bce_war[jg], bv_war.get(jg - 1)])
                    mul_war[jg] = t_mul
                    t_bv = p.op("act", lambda E, jj=j, cc=c:
                                E.activation(
                                    attnT[64 * (jj % 2): 64 * (jj % 2) + 64,
                                          (jj // 2) * 2048 + cc * 512: (jj // 2) * 2048 + (cc + 1) * 512],
                                    TMP, AF.Identity,
                                    bias=bvb[64 * (jj % 2): 64 * (jj % 2) + 64, jj // 2: jj // 2 + 1]),
                                deps=[t_mul])
                    bv_war[jg] = t_bv

                jg = 0
                for c in range(NCH):
                    dep_prev = [qtc_reader.get(c - 2)] + qst_all[-2:]
                    t_ld = None
                    for j in range(NH):
                        h = j // G
                        t_ld = p.dma("sp", f"qtc{c % 2}",
                                     qtc_ap(c % 2, tau(j), 64 * (h % 2), 64 * (h % 2) + 64),
                                     qts[(j // 2) * 128 + 64 * (j % 2): (j // 2) * 128 + 64 * (j % 2) + 64,
                                         c * 512:(c + 1) * 512],
                                     deps=dep_prev)
                    qtc_tok[c] = t_ld
                    for j in range(NH):
                        p3_qse(c, j, jg)
                        if jg >= 4:
                            p3_mulbv(jg - 4)
                        if jg >= 2:
                            p3_pvr(jg - 2)
                        if jg >= 3:
                            p3_bcb(jg - 3)
                        jg += 1
                p3_mulbv(jg - 4)
                p3_pvr(jg - 2)
                p3_bcb(jg - 3)
                p3_mulbv(jg - 3)
                p3_pvr(jg - 1)
                p3_bcb(jg - 2)
                p3_mulbv(jg - 2)
                p3_bcb(jg - 1)
                p3_mulbv(jg - 1)
                t_attn_done = [bv_war[jg - 1], bv_war[jg - 2]]
                p.flush()

            if "p4" in stages:
                p.start_phase(f"p4_{b}")
                WOB = BIG2[:, WO0:WO0 + 32768]
                ZN = LNOUT
                wo_f = []
                for dqt in range(NT):
                    t_ = p.dma("pool", f"wo{dqt % 2}", WOB[:, dqt * 2048:(dqt + 1) * 2048],
                               wot[dqt * 128:(dqt + 1) * 128, :])
                    if dqt >= NT - 2:
                        wo_f.append(t_)
                lnst = {}
                tile_done = {}
                st4 = {}
                pend4 = {}
                state4 = {"prev_app": None}
                _n4 = min(NT, P4LIM)

                def p4_stageA(nt):
                    sl = nt % 2
                    psum = PSA if sl == 0 else PSB
                    freed = tile_done.get(nt - 2, [])
                    t_mm = None
                    for dc in range(4):
                        for dqt in range(NT):
                            t_mm = p.op("pe", lambda E, d=dc, q2=dqt, n=nt, ps=psum:
                                        E.matmul(ps[:, d * 512:(d + 1) * 512],
                                                 attnT[:, q2 * 2048 + n * 128: q2 * 2048 + n * 128 + 128],
                                                 WOB[:, q2 * 2048 + d * 512: q2 * 2048 + (d + 1) * 512],
                                                 start=(q2 == 0), stop=(q2 == NT - 1)),
                                        deps=(wo_f + t_attn_done + freed) if (dc == 0 and dqt == 0) else [],
                                        inc=(dqt == NT - 1 and dc == 3))
                    t_x = p.dma("sp", f"xl{sl}", xf(sl)[:, :],
                                xin[row0 + nt * 128: row0 + (nt + 1) * 128, :],
                                deps=freed + [st4.get(nt - 2)])
                    pend4[nt] = (emit_ln_stats(psum[:, :], sl, t_mm, freed, lnst), t_x, psum)

                def p4_stageB(nt):
                    sl = nt % 2
                    (t_sum, t_sq, t_rstd, t_nmr), t_x, psum = pend4[nt]
                    t_app = p.op("act", lambda E, s=sl, ps=psum:
                                 E.activation(ZN, ps[:, :], AF.Identity,
                                              bias=stats(s)[:, 6:7], scale=stats(s)[:, 5:6]),
                                 deps=[t_rstd, t_nmr, state4["prev_app"], st4.get(nt - 1)])
                    state4["prev_app"] = t_app
                    t_add = p.op("dve", lambda E, s=sl: E.tensor_add(ZN, ZN, xf(s)[:, :]),
                                 deps=[t_app, t_x])
                    t_s = p.dma("sp", f"xm{sl}",
                                xmid[row0 + nt * 128: row0 + (nt + 1) * 128, :], ZN,
                                deps=[t_add])
                    st4[nt] = t_s
                    tile_done[nt] = [t_app, t_sum]

                p4_stageA(0)
                for nt in range(1, _n4):
                    p4_stageB(nt - 1)
                    p4_stageA(nt)
                p4_stageB(_n4 - 1)
                if _n4 >= 2:
                    t_xmid_fin += [st4[_n4 - 1], st4[_n4 - 2]]
                p.flush()

        if "p5" in stages:
            p.start_phase("p5")
            NCHT = (nbatch * SEQ) // 512
            N5 = min(NCHT, P5LIM)
            hidT = BIG
            Z2B = BIG2[:, Z20:Z20 + 8192]
            XN2T = BIG2[:, XN20:XN20 + 8192]
            W1B = [BIG2[:, W1B0 + i * 4096: W1B0 + (i + 1) * 4096] for i in range(2)]
            W3B = [BIG2[:, W3B0 + i * 4096: W3B0 + (i + 1) * 4096] for i in range(2)]
            W2PS = [B3B[:, W2PS0 + i * 1024: W2PS0 + (i + 1) * 1024] for i in range(3)]
            XN2S = B3B[:, XN2S0:XN2S0 + 8192]

            lnst = {}
            ln2_app = {}
            ln2_done = {}
            hid_war = {}
            w2_rd = {}
            dwn_ev = {}
            out_sts = {}
            up_last = {}
            pb_war = {}
            ln3_done = {}
            pend5 = {}
            xn2_tok = {}
            st5 = {"prev_app2": None, "prev_app3": None, "prev_add3": None, "k5": 0}

            tr4_done = {}

            def ln2_stats(g):
                for lt in range(4):
                    nt = g * 4 + lt
                    sl = lt % 2
                    freed = ln2_done.get((g, lt - 2), []) + ln2_done.get((g - 1, lt + 2), [])
                    t_x = p.dma("sp", f"x5{sl}", xf(sl)[:, :],
                                xmid[nt * 128:(nt + 1) * 128, :],
                                deps=freed + t_xmid_fin[-2:])
                    t_sum, t_sq, t_rstd, t_nmr = emit_ln_stats(xf(sl)[:, :], sl, t_x, freed, lnst)
                    t_app = p.op("act", lambda E, s=sl, l=lt:
                                 E.activation(XN2S[:, l * 2048:(l + 1) * 2048], xf(s)[:, :],
                                              AF.Identity,
                                              bias=stats(s)[:, 6:7], scale=stats(s)[:, 5:6]),
                                 deps=[t_rstd, t_nmr, st5["prev_app2"],
                                       tr4_done.get((g - 1, lt))])
                    st5["prev_app2"] = t_app
                    ln2_app[(g, lt)] = t_app
                    ln2_done[(g, lt)] = [t_app, t_sum]

            def ln2_apply_tr(g, free_dep, updep):
                k5 = st5["k5"]
                for lt in range(4):
                    t_ev = None
                    for d4 in range(4):
                        bk = k5 % 4
                        t_tr = None
                        for di in range(4):
                            dt = d4 * 4 + di
                            t_tr = p.op("pe", lambda E, d=dt, l=lt, b2=bk, i2=di:
                                        E.matmul(PB(b2)[:, i2 * 128:(i2 + 1) * 128],
                                                 XN2S[:, l * 2048 + d * 128: l * 2048 + (d + 1) * 128],
                                                 idb[:, :], start=True, stop=True),
                                        deps=[ln2_app[(g, lt)], pb_war.get(k5 - 4),
                                              updep, free_dep] if di == 0 else [],
                                        inc=(di == 3))
                        t_ev = p.op("dve", lambda E, d0=d4 * 4, l=lt, b2=bk:
                                    E.tensor_copy(
                                        bass.AP(BIG2, XN20 + d0 * 512 + l * 128,
                                                [[32768, 128], [512, 4], [1, 128]]),
                                        bass.AP(PSB, b2 * 512, [[2048, 128], [128, 4], [1, 128]])),
                                    deps=[t_tr])
                        pb_war[k5] = t_ev
                        k5 += 1
                    tr4_done[(g, lt)] = t_ev
                st5["k5"] = k5
                xn2_tok[g] = [pb_war[k5 - 1], pb_war[k5 - 2]]

            def up(g):
                t_hl = None
                for hb in range(32):
                    wsl = hb % 2
                    wfree = [hid_war.get((g, (hb - 2) * 2 + 1)),
                             hid_war.get((g - 1, 60 + (hb % 2) * 2 + 1)) if hb < 2 else None]
                    t_w1 = p.dma("sp", f"w1s{wsl}", W1B[wsl][:, :], w1tb[hb, :, :], deps=wfree)
                    t_w3 = p.dma("sp", f"w3s{wsl}", W3B[wsl][:, :], w3tb[hb, :, :], deps=wfree)
                    for hl in range(2):
                        ht = hb * 2 + hl
                        bkg = (ht % 2) * 2
                        free_h = [hid_war.get((g, ht - 2)),
                                  dwn_ev.get((g - 1, 1, 7)) if ht < 2 else None]
                        t_g = None
                        t_l = None
                        for dt in range(NT):
                            t_g = p.op("pe", lambda E, s=wsl, d=dt, h2=hl, b2=bkg:
                                       E.matmul(PA(b2), W1B[s][:, d * 256 + h2 * 128: d * 256 + h2 * 128 + 128],
                                                XN2T[:, d * 512:(d + 1) * 512],
                                                start=(d == 0), stop=(d == NT - 1)),
                                       deps=([t_w1] + xn2_tok[g] + free_h) if dt == 0 else [],
                                       inc=(dt == NT - 1))
                            t_l = p.op("pe", lambda E, s=wsl, d=dt, h2=hl, b2=bkg + 1:
                                       E.matmul(PA(b2), W3B[s][:, d * 256 + h2 * 128: d * 256 + h2 * 128 + 128],
                                                XN2T[:, d * 512:(d + 1) * 512],
                                                start=(d == 0), stop=(d == NT - 1)),
                                       deps=([t_w3] + free_h) if dt == 0 else [],
                                       inc=(dt == NT - 1))
                        t_si = p.op("act", lambda E, b2=bkg:
                                    E.activation(SIL1, PA(b2), AF.Silu),
                                    deps=[t_g, hid_war.get((g, ht - 1)), hid_war.get((g, ht - 2))])
                        t_h = p.op("dve", lambda E, b2=bkg + 1, h2=ht:
                                   E.tensor_mul(hidT[:, h2 * 512:(h2 + 1) * 512], SIL1, PA(b2)),
                                   deps=[t_si, t_l, dwn_ev.get((g - 1, 1, 7))])
                        hid_war[(g, ht)] = t_h
                        t_hl = t_l
                up_last[g] = t_hl

            def downpass(g, pi):
                t_hid = hid_war[(g, 63)]
                for ht in range(64):
                    wsl = ht % 3
                    prev_ht = [63, 61, 62][ht] if ht < 3 else ht - 3
                    wprev = w2_rd.get((pi, prev_ht)) if ht >= 3 else w2_rd.get((1 - pi, prev_ht))
                    t_w2 = p.dma("sp", f"w2s{wsl}", W2PS[wsl][:, :],
                                 w2pt[ht * 128:(ht + 1) * 128, pi * 1024:(pi + 1) * 1024],
                                 deps=[wprev])
                    t_rd = None
                    for nt in range(4):
                        for dh in range(2):
                            bank8 = nt * 2 + dh
                            psum = PSA if bank8 < 4 else PSB
                            bk = bank8 % 4
                            dep0 = [t_w2]
                            if ht == 0:
                                dep0 += [t_hid, pb_war.get(st5["k5"] - 1),
                                         dwn_ev.get((g, 0, 7)) if pi == 1 else dwn_ev.get((g - 1, 1, 7))]
                            t_rd = p.op("pe", lambda E, s=wsl, n=nt, d=dh, h2=ht, ps=psum, b2=bk:
                                        E.matmul(ps[:, b2 * 512:(b2 + 1) * 512],
                                                 hidT[:, h2 * 512 + n * 128: h2 * 512 + n * 128 + 128],
                                                 W2PS[s][:, d * 512:(d + 1) * 512],
                                                 start=(h2 == 0), stop=(h2 == 63)),
                                        deps=dep0,
                                        inc=(nt == 3 and dh == 1))
                    w2_rd[(pi, ht)] = t_rd

            def downevict(g, pi):
                for nt in range(4):
                    for dh in range(2):
                        bank8 = nt * 2 + dh
                        psum = PSA if bank8 < 4 else PSB
                        bk = bank8 % 4
                        t_z = p.op("dve", lambda E, ps=psum, b2=bk, n=nt, d=dh, pp=pi:
                                   E.tensor_copy(Z2B[:, n * 2048 + pp * 1024 + d * 512:
                                                     n * 2048 + pp * 1024 + (d + 1) * 512],
                                                 ps[:, b2 * 512:(b2 + 1) * 512]),
                                   deps=[w2_rd[(pi, 63)]] + (ln3_done.get((g - 1, nt)) or []))
                        dwn_ev[(g, pi, nt * 2 + dh)] = t_z

            def ln3(g):
                for lt in range(4):
                    nt = g * 4 + lt
                    sl = 2 + lt % 2
                    zsrc = BIG2[:, Z20 + lt * 2048: Z20 + (lt + 1) * 2048]
                    t_x = p.dma("act", "x6", XF6,
                                xmid[nt * 128:(nt + 1) * 128, :],
                                deps=[st5["prev_add3"]] + t_xmid_fin[-2:])
                    zdeps = [dwn_ev[(g, 0, lt * 2)], dwn_ev[(g, 0, lt * 2 + 1)],
                             dwn_ev[(g, 1, lt * 2)], dwn_ev[(g, 1, lt * 2 + 1)]]
                    t_sum, t_sq, t_rstd, t_nmr = emit_ln_stats(
                        zsrc, sl, zdeps[0],
                        zdeps[1:] + (ln3_done.get((g, lt - 2)) or []), lnst)
                    t_app = p.op("act", lambda E, s=sl, z=zsrc:
                                 E.activation(LNOUT, z, AF.Identity,
                                              bias=stats(s)[:, 6:7], scale=stats(s)[:, 5:6]),
                                 deps=[t_rstd, t_nmr, st5["prev_app3"],
                                       out_sts.get((g, lt - 1)),
                                       out_sts.get((g - 1, 3)) if lt == 0 else None])
                    st5["prev_app3"] = t_app
                    t_add = p.op("dve", lambda E: E.tensor_add(LNOUT, LNOUT, XF6),
                                 deps=[t_app, t_x])
                    st5["prev_add3"] = t_add
                    t_s = p.dma("act", f"oo{lt % 2}", out[nt * 128:(nt + 1) * 128, :], LNOUT,
                                deps=[t_add])
                    out_sts[(g, lt)] = t_s
                    ln3_done[(g, lt)] = [t_app, t_sum]

            ln2_stats(0)
            ln2_apply_tr(0, None, None)
            for g_ in range(N5):
                if g_ + 1 < N5:
                    ln2_stats(g_ + 1)
                up(g_)
                if g_ + 1 < N5:
                    ln2_apply_tr(g_ + 1, dwn_ev.get((g_ - 1, 1, 7)), up_last[g_])
                downpass(g_, 0)
                downevict(g_, 0)
                downpass(g_, 1)
                downevict(g_, 1)
                ln3(g_)

            fin = [out_sts[(N5 - 1, lt)] for lt in range(4)]
            p.op("sp", lambda E: E.nop(), deps=fin)
            p.flush()
        else:
            p.start_phase("pcopy")
            prevc = {}
            for nt in range(TOK // 128):
                sl = nt % 2
                tl = p.dma("sp", f"cl{sl}", xf(sl)[:, :],
                           xin[nt * 128:(nt + 1) * 128, :], deps=[prevc.get(nt - 2)])
                ts = p.dma("sp", f"co{sl}", out[nt * 128:(nt + 1) * 128, :],
                           xf(sl)[:, :], deps=[tl])
                prevc[nt] = ts
            p.op("sp", lambda E: E.nop(),
                 deps=[prevc[TOK // 128 - 1], prevc[TOK // 128 - 2]])
            p.flush()

    return nc


def _hadamard(n):
    H = np.array([[1.0]], dtype=np.float64)
    while H.shape[0] < n:
        H = np.block([[H, H], [H, -H]])
    return H


def make_inputs(x, Wq, bq, Wk, bk, Wv, bv, Wo, W1, W2, W3, ncore=8):
    f32 = np.float32
    bf16 = ml_dtypes.bfloat16
    x = np.ascontiguousarray(np.asarray(x, dtype=f32))
    B = x.shape[0]
    bq = np.asarray(bq, f32)
    bk = np.asarray(bk, f32)
    bv = np.asarray(bv, f32)
    bvc = np.zeros((128, 16), f32)
    for jt in range(16):
        bvc[0:64, jt] = bv[64 * ((2 * jt) // G): 64 * ((2 * jt) // G) + 64]
        bvc[64:128, jt] = bv[64 * ((2 * jt + 1) // G): 64 * ((2 * jt + 1) // G) + 64]
    pcol = np.arange(128)[:, None]
    u = np.arange(RW)[None, :]
    shared = dict(
        wqt=np.ascontiguousarray(np.asarray(Wq, f32).T),
        wkt=np.ascontiguousarray(np.asarray(Wk, f32).T),
        wvt=np.ascontiguousarray(np.asarray(Wv, f32).T),
        wot=np.ascontiguousarray(np.asarray(Wo, f32).T),
        w1t=np.ascontiguousarray(np.asarray(W1, f32).T),
        w3t=np.ascontiguousarray(np.asarray(W3, f32).T),
        w2t=np.ascontiguousarray(np.asarray(W2, f32).T),
        bqc=np.ascontiguousarray(bq.reshape(16, 128).T),
        bkc=np.ascontiguousarray(bk.reshape(4, 128).T),
        bvc=bvc,
        rbase=(-np.abs(u - 1920 - pcol)).astype(bf16),
        h128=_hadamard(128).astype(bf16),
        h64=_hadamard(64).astype(bf16),
        identin=np.eye(128, dtype=bf16),
        onesin=np.ones((1, 64), dtype=bf16),
    )
    bpc = B // ncore
    in_maps = []
    for c in range(ncore):
        m = dict(shared)
        m["xin"] = np.ascontiguousarray(x[c * bpc:(c + 1) * bpc].reshape(bpc * SEQ, DM))
        in_maps.append(m)
    return in_maps, bpc


def kernel(L0_attn_hidden_states, L0_attn_Wq, L0_attn_bq, L0_attn_Wk, L0_attn_bk,
           L0_attn_Wv, L0_attn_bv, L0_attn_Wo, L0_ffn_W1, L0_ffn_W2, L0_ffn_W3):
    in_maps, bpc = make_inputs(L0_attn_hidden_states, L0_attn_Wq, L0_attn_bq,
                               L0_attn_Wk, L0_attn_bk, L0_attn_Wv, L0_attn_bv,
                               L0_attn_Wo, L0_ffn_W1, L0_ffn_W2, L0_ffn_W3)
    nc = build(bpc)
    res = run_bass_kernel_spmd(nc, in_maps, core_ids=list(range(8)))
    outs = [r["out"].reshape(bpc, SEQ, DM) for r in res.results]
    return np.concatenate(outs, axis=0)
